# version 19 (final)
# speedup vs baseline: 1.3826x; 1.0451x over previous
import numpy as np
from contextlib import ExitStack
import concourse.bass as bass
import concourse.mybir as mybir
from concourse.bass_utils import run_bass_kernel_spmd

F32 = mybir.dt.float32
BF16 = mybir.dt.bfloat16
AF = mybir.ActivationFunctionType
ALU = mybir.AluOpType
AX = mybir.AxisListType

NCORE = 8
D = 1024
T = 2048
DC = 8
TC = 4
NT = 16
LN_EPS = 1e-5
L1_LN_EPS = 1e-6

ALL_BLOCKS = ("l0a", "l0f", "l1a", "l1f")



def _pm(v):
    v = np.ascontiguousarray(np.asarray(v, dtype=np.float32)).reshape(-1)
    k = v.shape[0] // 128
    return np.ascontiguousarray(v.reshape(k, 128).T)


def _pm2(v):
    v = np.asarray(v, dtype=np.float32)
    e, n = v.shape
    k = n // 128
    return np.ascontiguousarray(v.reshape(e, k, 128).transpose(2, 0, 1).reshape(128, e * k))


def _consts():
    import ml_dtypes
    c = {}
    c["c_ident"] = np.eye(128, dtype=np.float32)
    c["c_nidb"] = (-np.eye(128)).astype(ml_dtypes.bfloat16)
    c["c_ones"] = np.ones((128, 128), dtype=np.float32)
    c["c_onesb"] = np.ones((1, 128), dtype=ml_dtypes.bfloat16)
    c["c_lnred"] = np.full((128, 1), 1.0 / 1024.0, dtype=ml_dtypes.bfloat16)
    c["c_eps"] = np.array([[LN_EPS, L1_LN_EPS]], dtype=np.float32)
    h0 = np.zeros((8, 128, 16), dtype=np.float32)
    for cc in range(8):
        for p in range(128):
            h0[cc, p, 2 * cc + p // 64] = 1.0
    c["c_hselL0"] = h0
    h1 = np.zeros((4, 128, 8), dtype=np.float32)
    for cc in range(4):
        for p in range(128):
            h1[cc, p, 2 * cc + p // 64] = 0.125
    c["c_hselL1"] = h1
    s1 = np.zeros((8, 4, 128), dtype=np.float32)
    for cc in range(4):
        for p in range(128):
            s1[2 * cc + p // 64, cc, p] = 1.0
    c["c_selL1"] = np.ascontiguousarray(s1.reshape(8, 4 * 128))
    sE = np.zeros((8, 8 * 128), dtype=np.float32)
    for e in range(8):
        sE[e, e * 128:(e + 1) * 128] = 1.0
    c["c_selE"] = sE.astype(ml_dtypes.bfloat16)
    return c


def _prep_shared(inputs, blocks=ALL_BLOCKS):
    asc = np.ascontiguousarray
    g = lambda k: np.asarray(inputs[k], dtype=np.float32)
    sh = {}
    if "l0a" in blocks:
        wqkvT = asc(g("L0_attn_w_qkv").T)
        sh["w_qkvT"] = asc(wqkvT.reshape(1024, 24, 128).transpose(1, 0, 2))
        woutT = asc(g("L0_attn_w_out").T)
        sh["w_outT"] = asc(woutT.reshape(1024, 8, 128).transpose(1, 0, 2))
    if "l0f" in blocks:
        r = g("L0_ffn_router_kernel_DE")
        sh["router"] = asc(r.reshape(8, 128, 8).transpose(1, 0, 2).reshape(128, 64))
        sh["gating"] = g("L0_ffn_kernel_gating_EDF")
        sh["up"] = g("L0_ffn_kernel_up_EDF")
        sh["down"] = g("L0_ffn_kernel_down_EFD")
    if "l1a" in blocks:
        sh["wq1"] = g("L1_attn_Wq")
        sh["wk1"] = g("L1_attn_Wk")
        sh["wv1"] = g("L1_attn_Wv")
        sh["wo1"] = g("L1_attn_Wo")
        sh["bq1"] = _pm(g("L1_attn_bq"))
        sh["bk1"] = _pm(g("L1_attn_bk"))
        sh["bv1"] = _pm(g("L1_attn_bv"))
        sh["bo1"] = _pm(g("L1_attn_bo"))
        sh["alpha1"] = _pm(g("L1_attn_alpha"))
        sh["augp1"] = _pm(g("L1_attn_aug_param"))
        sh["lnsc1"] = _pm(g("L1_attn_ln_scale"))
        sh["lnbs1"] = _pm(g("L1_attn_ln_bias"))
    if "l1f" in blocks:
        sh["wbase"] = g("L1_ffn_W_base")
        sh["bbase"] = _pm(g("L1_ffn_b_base"))
        sh["wd1"] = g("L1_ffn_Wd1")
        sh["bd1"] = _pm2(g("L1_ffn_bd1"))
        sh["wd2"] = g("L1_ffn_Wd2")
        sh["bd2"] = _pm2(g("L1_ffn_bd2"))
        sh["bd2t"] = asc(g("L1_ffn_bd2").reshape(64, 128))
        sh["wad"] = g("L1_ffn_Wa_down")
        sh["bad"] = asc(g("L1_ffn_ba_down").T)
        sh["wau"] = g("L1_ffn_Wa_up")
        sh["bau"] = _pm2(g("L1_ffn_ba_up"))
    sh.update(_consts())
    return sh



class Sem:
    __slots__ = ("h", "n")

    def __init__(self, ctx, nc, name):
        self.h = ctx.enter_context(nc.semaphore(name))
        self.n = 0

    def dma(self, ins):
        ins.then_inc(self.h, 16)
        self.n += 16
        return self.n

    def inc(self, ins, k=1):
        ins.then_inc(self.h, k)
        self.n += k
        return self.n


class Env:
    def vs(self, eng, ins):
        s = self.sem["vs"]
        s.inc(ins)
        eng.wait_ge(s.h, s.n)

    def asy(self, eng, ins):
        s = self.sem["as"]
        s.inc(ins)
        eng.wait_ge(s.h, s.n)


_DRAM_SPECS = {
    "xin": ((1024, 2048), F32),
    "c_ident": ((128, 128), F32), "c_ones": ((128, 128), F32),
    "c_lnred": ((128, 1), BF16), "c_onesb": ((1, 128), BF16),
    "c_eps": ((1, 2), F32),
    "w_qkvT": ((24, 1024, 128), F32), "w_outT": ((8, 1024, 128), F32),
    "c_hselL0": ((8, 128, 16), F32),
    "router": ((128, 64), F32), "gating": ((8, 1024, 2048), F32),
    "up": ((8, 1024, 2048), F32), "down": ((8, 2048, 1024), F32),
    "c_selE": ((8, 1024), BF16),
    "wq1": ((1024, 512), F32), "wk1": ((1024, 512), F32), "wv1": ((1024, 512), F32),
    "wo1": ((512, 1024), F32), "bq1": ((128, 4), F32), "bk1": ((128, 4), F32),
    "bv1": ((128, 4), F32), "bo1": ((128, 8), F32), "alpha1": ((128, 4), F32),
    "augp1": ((128, 4), F32), "lnsc1": ((128, 8), F32), "lnbs1": ((128, 8), F32),
    "c_hselL1": ((4, 128, 8), F32), "c_selL1": ((8, 512), F32),
    "c_nidb": ((128, 128), BF16), "bd2t": ((64, 128), F32),
    "wbase": ((1024, 1024), F32), "bbase": ((128, 8), F32),
    "wd1": ((8, 1024, 256), F32), "bd1": ((128, 16), F32),
    "wd2": ((8, 256, 1024), F32), "bd2": ((128, 64), F32),
    "wad": ((8, 1024, 64), F32), "bad": ((64, 8), F32),
    "wau": ((8, 64, 1024), F32), "bau": ((128, 64), F32),
}

_BLOCK_INPUTS = {
    "l0a": ["w_qkvT", "w_outT", "c_hselL0"],
    "l0f": ["router", "gating", "up", "down", "c_selE"],
    "l1a": ["wq1", "wk1", "wv1", "wo1", "bq1", "bk1", "bv1", "bo1",
            "alpha1", "augp1", "lnsc1", "lnbs1", "c_hselL1", "c_selL1"],
    "l1f": ["wbase", "bbase", "wd1", "bd1", "wd2", "bd2", "wad", "bad",
            "wau", "bau", "c_nidb", "bd2t"],
}


def build_program(blocks=ALL_BLOCKS, debug=()):
    nc = bass.Bass("TRN2", target_bir_lowering=False)
    env = Env()
    env.nc = nc
    env.uid = [0]
    env.debug = set(debug)
    env.ndbg = [0]
    names = ["xin", "c_ident", "c_ones", "c_lnred", "c_onesb", "c_eps"]
    for bkey in blocks:
        names += _BLOCK_INPUTS[bkey]
    env.dram = {k: nc.dram_tensor(k, list(_DRAM_SPECS[k][0]), _DRAM_SPECS[k][1],
                                  kind="ExternalInput") for k in names}
    env.y = nc.dram_tensor("y", [1024, 2048], F32, kind="ExternalOutput")
    with ExitStack() as ctx:
        env.ctx = ctx
        env.sem = {k: Sem(ctx, nc, "s_" + k) for k in
                   ["ld", "w", "c", "m", "ev", "v", "a", "p", "e", "x",
                    "s2", "e2", "pv", "lr", "li", "lb", "oe", "v2", "st",
                    "vs", "as"]}
        sb = lambda name, shape, dt: ctx.enter_context(nc.sbuf_tensor(name, shape, dt))
        env.x = sb("x", [128, DC, T], F32)
        env.xln = sb("xln", [128, DC, T], BF16)
        env.ident = sb("ident", [128, 128], F32)
        env.ones = sb("ones", [128, 128], F32)
        env.onesb = sb("onesb", [1, 128], BF16)
        env.lnred = sb("lnred", [128, 1], BF16)
        env.epsc = sb("epsc", [1, 2], F32)
        if "l0a" in blocks:
            env.hselL0 = sb("hselL0", [128, 8, 16], F32)
        if "l0f" in blocks:
            env.selE = sb("selE", [8, 1024], BF16)
        if "l1a" in blocks:
            env.hselL1 = sb("hselL1", [128, 4, 8], F32)
            env.selL1 = sb("selL1", [8, 512], F32)
            for k in ["bq1", "bk1", "bv1", "bo1", "alpha1", "augp1",
                      "lnsc1", "lnbs1"]:
                setattr(env, k, sb("sb_" + k, list(_DRAM_SPECS[k][0]), F32))
        if "l1f" in blocks:
            for k in ["bbase", "bd1", "bd2", "bad", "bau"]:
                setattr(env, k, sb("sb_" + k, list(_DRAM_SPECS[k][0]), F32))
            env.nidb = sb("sb_nidb", [128, 128], BF16)
        emit_prologue(env, blocks)
        if "l0a" in blocks:
            emit_ln(env, LN_EPS)
            emit_dump(env, "xln1", env.xln[:, :, :], ((128, DC, T), BF16))
            emit_l0_attn(env)
        if "l0f" in blocks:
            emit_ln(env, LN_EPS)
            emit_l0_ffn(env)
        if "l1a" in blocks:
            emit_ln(env, LN_EPS)
            emit_l1_attn(env)
        if "l1f" in blocks:
            emit_ln(env, LN_EPS)
            emit_l1_ffn(env)
        emit_store(env)
    return nc


def emit_dump(env, name, ap, shape):
    if name not in env.debug:
        return
    nc = env.nc
    env.ndbg[0] += 1
    t = nc.dram_tensor("dbg_" + name, list(shape[0]), shape[1], kind="ExternalOutput")
    sS = env.sem["st"]
    with nc.Block() as b:
        @b.sync
        def _(sy):
            sS.dma(sy.dma_start(t[...], ap))
            sy.wait_ge(sS.h, sS.n)


def emit_prologue(env, blocks):
    nc = env.nc
    d = env.dram
    small = [("c_ident", env.ident), ("c_ones", env.ones),
             ("c_lnred", env.lnred), ("c_onesb", env.onesb), ("c_eps", env.epsc)]
    if "l0a" in blocks:
        small += [("c_hselL0", env.hselL0)]
    if "l0f" in blocks:
        small += [("c_selE", env.selE)]
    if "l1a" in blocks:
        small += [("c_hselL1", env.hselL1), ("c_selL1", env.selL1)]
        small += [(k, getattr(env, k)) for k in
                  ["bq1", "bk1", "bv1", "bo1", "alpha1", "augp1", "lnsc1", "lnbs1"]]
    if "l1f" in blocks:
        small += [(k, getattr(env, k)) for k in ["bbase", "bd1", "bd2", "bad", "bau"]]
        small += [("c_nidb", env.nidb)]
    sL = env.sem["ld"]
    with nc.Block() as b:
        @b.sync
        def _(sy):
            for dc in range(DC):
                sL.dma(sy.dma_start(env.x[:, dc, :],
                                    d["xin"][dc * 128:(dc + 1) * 128, :]))
            for name, t in small:
                src = d[name]
                if len(src.shape) == 3:
                    ap = src[:, :, :].rearrange("n p f -> p n f")
                else:
                    ap = src[:, :]
                sL.dma(sy.dma_start(t[...], ap))
            sy.wait_ge(sL.h, sL.n)


def emit_store(env):
    nc = env.nc
    sS = env.sem["st"]
    with nc.Block() as b:
        @b.sync
        def _(sy):
            for dc in range(DC):
                sS.dma(sy.dma_start(env.y[dc * 128:(dc + 1) * 128, :],
                                    env.x[:, dc, :]))
            sy.wait_ge(sS.h, sS.n)



def emit_ln(env, eps):
    nc = env.nc
    env.uid[0] += 1
    u = str(env.uid[0])
    x, xln = env.x, env.xln
    sV, sA, sP, sE = env.sem["v"], env.sem["a"], env.sem["p"], env.sem["e"]
    with ExitStack() as c:
        sb = lambda name, shape, dt: c.enter_context(nc.sbuf_tensor(name + u, shape, dt))
        sq = sb("ln_sq", [128, DC, 512], BF16)
        rmean = sb("ln_rmean", [1, T], F32)
        rsq = sb("ln_rsq", [1, T], F32)
        rtmp = sb("ln_rtmp", [1, T], F32)
        rrstd = sb("ln_rrstd", [1, T], F32)
        rnmr = sb("ln_rnmr", [1, T], F32)
        sc1 = sb("ln_sc1", [128, 512], F32)
        with ExitStack() as cp:
            pmean = cp.enter_context(nc.psum_tensor("ln_pm" + u, [1, 512], F32))
            psq = cp.enter_context(nc.psum_tensor("ln_ps" + u, [1, 512], F32))
            V0, A0, P0 = sV.n, sA.n, sP.n
            with nc.Block() as b:
                @b.vector
                def _(ve):
                    for dc in range(DC):
                        sV.inc(ve.tensor_copy(xln[:, dc, :], x[:, dc, :]))
                    for tc in range(TC):
                        ve.wait_ge(sP.h, P0 + 2 * tc + 2)
                        ve.tensor_copy(rmean[:, tc * 512:(tc + 1) * 512], pmean[:, :])
                        sV.inc(ve.tensor_copy(rsq[:, tc * 512:(tc + 1) * 512], psq[:, :]))

                @b.scalar
                def _(se):
                    for tc in range(TC):
                        if tc > 0:
                            se.wait_ge(sP.h, P0 + 2 * tc)
                        for dc in range(DC):
                            if tc == 0:
                                se.wait_ge(sV.h, V0 + dc + 1)
                            sA.inc(se.activation(sq[:, dc, :],
                                                 xln[:, dc, tc * 512:(tc + 1) * 512],
                                                 AF.Square))

                @b.tensor
                def _(te):
                    te.wait_ge(sV.h, V0 + 8)
                    for tc in range(TC):
                        if tc > 0:
                            te.wait_ge(sV.h, V0 + 8 + tc)
                        for dc in range(DC):
                            ins = te.matmul(pmean[:, :], env.lnred[:, 0:1],
                                            xln[:, dc, tc * 512:(tc + 1) * 512],
                                            start=(dc == 0), stop=(dc == DC - 1))
                        sP.inc(ins)
                        for dc in range(DC):
                            te.wait_ge(sA.h, A0 + tc * 8 + dc + 1)
                            ins = te.matmul(psq[:, :], env.lnred[:, 0:1], sq[:, dc, :],
                                            start=(dc == 0), stop=(dc == DC - 1))
                        sP.inc(ins)
            assert sV.n == V0 + 12 and sA.n == A0 + 32 and sP.n == P0 + 8
            V0, A0 = sV.n, sA.n
            with nc.Block() as b:
                @b.vector
                def _(ve):
                    ve.tensor_mul(rtmp[:, :], rmean[:, :], rmean[:, :])
                    sV.inc(ve.tensor_sub(rsq[:, :], rsq[:, :], rtmp[:, :]))
                    ve.tensor_scalar_mul(rnmr[:, :], rmean[:, :], -1.0)

                @b.scalar
                def _(se):
                    se.wait_ge(sV.h, V0 + 1)
                    se.activation(rtmp[:, :], rsq[:, :], AF.Ln, bias=env.epsc[0:1, 0:1])
                    sA.inc(se.activation(rrstd[:, :], rtmp[:, :], AF.Exp, scale=-0.5))
        with ExitStack() as cp:
            pb = [cp.enter_context(nc.psum_tensor(f"ln_pb{i}" + u, [128, 512], F32))
                  for i in range(4)]
            sbb = c.enter_context(nc.sbuf_tensor("ln_sbb" + u, [128, 2, 2, 512], F32))
            sc1p = c.enter_context(nc.sbuf_tensor("ln_sc1p" + u, [128, 512], F32))
            P0, E0, G0 = sP.n, sE.n, env.sem["x"].n
            sG = env.sem["x"]
            with nc.Block() as b:
                @b.tensor
                def _(te):
                    for tc in range(TC):
                        if tc >= 2:
                            te.wait_ge(sE.h, E0 + tc - 1)
                        te.matmul(pb[2 * (tc % 2)][:, :], env.ones[0:1, :],
                                  rrstd[:, tc * 512:(tc + 1) * 512], start=True, stop=True)
                        sP.inc(te.matmul(pb[2 * (tc % 2) + 1][:, :], env.ones[0:1, :],
                                         rnmr[:, tc * 512:(tc + 1) * 512],
                                         start=True, stop=True))

                @b.vector
                def _(ve):
                    for tc in range(TC):
                        ve.wait_ge(sP.h, P0 + tc + 1)
                        if tc >= 2:
                            ve.wait_ge(sG.h, G0 + tc - 1)
                        ve.tensor_copy(sbb[:, tc % 2, 0, :], pb[2 * (tc % 2)][:, :])
                        sE.inc(ve.tensor_copy(sbb[:, tc % 2, 1, :],
                                              pb[2 * (tc % 2) + 1][:, :]))
                        for dc in range(4):
                            ve.tensor_add(sc1[:, :], x[:, dc, tc * 512:(tc + 1) * 512],
                                          sbb[:, tc % 2, 1, :])
                            ve.tensor_mul(xln[:, dc, tc * 512:(tc + 1) * 512],
                                          sc1[:, :], sbb[:, tc % 2, 0, :])

                @b.gpsimd
                def _(ge):
                    for tc in range(TC):
                        ge.wait_ge(sE.h, E0 + tc + 1)
                        for dc in range(4, DC):
                            ge.tensor_add(sc1p[:, :], x[:, dc, tc * 512:(tc + 1) * 512],
                                          sbb[:, tc % 2, 1, :])
                            ge.tensor_mul(xln[:, dc, tc * 512:(tc + 1) * 512],
                                          sc1p[:, :], sbb[:, tc % 2, 0, :])
                        sG.inc(ge.engine_nop())



def emit_l0_attn(env):
    nc = env.nc
    x, xln = env.x, env.xln
    d = env.dram
    S = env.sem
    for g in range(2):
        env.uid[0] += 1
        u = str(env.uid[0])
        with ExitStack() as c:
            sb = lambda name, shape, dt: c.enter_context(nc.sbuf_tensor(name + u, shape, dt))
            Qg = sb("a_Q", [128, 4, T], BF16)
            Kg = sb("a_K", [128, 4, T], BF16)
            Vaug = sb("a_V", [128, NT, 8, 65], BF16)
            Og = sb("a_O", [128, 4, T], BF16)
            sW, sC, sM, sEv = S["w"], S["c"], S["m"], S["ev"]
            with ExitStack() as cp:
                sbp = lambda name, shape, dt: cp.enter_context(nc.sbuf_tensor(name + u, shape, dt))
                stg = [sbp(f"a_stg{i}", [128, 1024], F32) for i in range(2)]
                wbf = [sbp(f"a_wbf{i}", [128, 1024], BF16) for i in range(2)]
                pQK = [cp.enter_context(nc.psum_tensor(f"a_pqk{i}" + u, [128, 512], F32))
                       for i in range(2)]
                W0, C0, M0, E0 = sW.n, sC.n, sM.n, sEv.n
                chunks = [4 * g + j for j in range(4)] + [8 + 4 * g + j for j in range(4)]
                with nc.Block() as b:
                    @b.sync
                    def _(sy):
                        for wi, blk in enumerate(chunks):
                            if wi >= 2:
                                sy.wait_ge(sC.h, C0 + wi - 1)
                            sW.dma(sy.dma_start(
                                stg[wi % 2][:, :].rearrange("p (dc j) -> p dc j", dc=8),
                                d["w_qkvT"][blk, :, :].rearrange("(dc p) j -> p dc j", p=128)))

                    @b.vector
                    def _(ve):
                        for wi in range(8):
                            ve.wait_ge(sW.h, W0 + 16 * (wi + 1))
                            if wi >= 2:
                                ve.wait_ge(sM.h, M0 + 4 * (wi - 1))
                            sC.inc(ve.tensor_copy(wbf[wi % 2][:, :], stg[wi % 2][:, :]))

                    @b.tensor
                    def _(te):
                        for wi in range(8):
                            te.wait_ge(sC.h, C0 + wi + 1)
                            for tc in range(TC):
                                qi = wi * 4 + tc
                                if qi >= 2:
                                    te.wait_ge(sEv.h, E0 + qi - 1)
                                for dc in range(DC):
                                    ins = te.matmul(pQK[qi % 2][:, :],
                                                    wbf[wi % 2][:, dc * 128:(dc + 1) * 128],
                                                    xln[:, dc, tc * 512:(tc + 1) * 512],
                                                    start=(dc == 0), stop=(dc == DC - 1))
                                sM.inc(ins)

                    @b.scalar
                    def _(se):
                        for wi in range(8):
                            dst = Qg if wi < 4 else Kg
                            ocl = wi % 4
                            for tc in range(TC):
                                qi = wi * 4 + tc
                                se.wait_ge(sM.h, M0 + qi + 1)
                                sEv.inc(se.copy(dst[:, ocl, tc * 512:(tc + 1) * 512],
                                                pQK[qi % 2][:, :]))

                    @b.gpsimd
                    def _(ge):
                        for tch in range(NT):
                            ge.memset(Vaug[:, tch, :, :], 1.0)
                assert sM.n == M0 + 32 and sEv.n == E0 + 32 and sC.n == C0 + 8
            with ExitStack() as cp:
                sbp = lambda name, shape, dt: cp.enter_context(nc.sbuf_tensor(name + u, shape, dt))
                pV = [cp.enter_context(nc.psum_tensor(f"a_pv{i}" + u, [128, 512], F32))
                      for i in range(2)]
                stg = [sbp(f"a_vstg{i}", [128, 1024], F32) for i in range(2)]
                vw = sbp("a_vw", [128, 4, 8, 128], BF16)
                W0, C0, M0, E0 = sW.n, sC.n, sM.n, sEv.n
                with nc.Block() as b:
                    @b.sync
                    def _(sy):
                        for j in range(4):
                            blk = 16 + 4 * g + j
                            if j >= 2:
                                sy.wait_ge(sC.h, C0 + j - 1)
                            sW.dma(sy.dma_start(
                                stg[j % 2][:, :].rearrange("p (dc j) -> p dc j", dc=8),
                                d["w_qkvT"][blk, :, :].rearrange("(dc p) j -> p dc j", p=128)))

                    @b.vector
                    def _(ve):
                        for j in range(4):
                            ve.wait_ge(sW.h, W0 + 16 * (j + 1))
                            sC.inc(ve.tensor_copy(
                                vw[:, j, :, :].rearrange("p a b -> p (a b)"),
                                stg[j % 2][:, :]))

                    @b.tensor
                    def _(te):
                        te.wait_ge(sC.h, C0 + 4)
                        for tch in range(NT):
                            if tch >= 2:
                                te.wait_ge(sEv.h, E0 + tch - 1)
                            for dc in range(DC):
                                ins = te.matmul(pV[tch % 2][:, :],
                                                xln[:, dc, tch * 128:(tch + 1) * 128],
                                                vw[:, :, dc, :],
                                                start=(dc == 0), stop=(dc == DC - 1))
                            sM.inc(ins)

                    @b.scalar
                    def _(se):
                        for tch in range(NT):
                            se.wait_ge(sM.h, M0 + tch + 1)
                            sEv.inc(se.copy(
                                Vaug[:, tch, :, 0:64],
                                pV[tch % 2][:, :].rearrange("p (h j) -> p h j", h=8)))
                assert sM.n == M0 + 16 and sEv.n == E0 + 16
            if g == 0:
                emit_dump(env, "Qg", Qg[:, :, :], ((128, 4, T), BF16))
                emit_dump(env, "Kg", Kg[:, :, :], ((128, 4, T), BF16))
                emit_dump(env, "Vaug", Vaug[:, :, :, :], ((128, NT, 8, 65), BF16))
            gbias = sb("a_gbias", [128, 1], F32)
            with ExitStack() as cp:
                pqn = [cp.enter_context(nc.psum_tensor(f"a_pqn{i}" + u, [16, 512], F32))
                       for i in range(4)]
                pb1 = cp.enter_context(nc.psum_tensor("a_pb1" + u, [128, 1], F32))
                sbp = lambda name, shape, dt: cp.enter_context(nc.sbuf_tensor(name + u, shape, dt))
                sqb = sbp("a_sqb", [128, T], BF16)
                qmax = sbp("a_qmax", [16, 1], F32)
                kmax = sbp("a_kmax", [16, 1], F32)
                mtmp = sbp("a_mtmp", [16, 1], F32)
                mrow = sbp("a_mrow", [1, 16], F32)
                gms = sbp("a_gms", [1, 4], F32)
                m8c = sbp("a_m8c", [16, 1], F32)
                hbf = sbp("a_hbf", [128, 4, 16], BF16)
                A0, M0, V0, X0, P0 = S["a"].n, sM.n, S["v2"].n, S["x"].n, S["p"].n
                sA, sV2, sX, sP = S["a"], S["v2"], S["x"], S["p"]
                with nc.Block() as b:
                    @b.vector
                    def _(ve):
                        for j in range(4):
                            ins = ve.tensor_copy(hbf[:, j, :], env.hselL0[:, 4 * g + j, :])
                        sV2.inc(ins)
                        for which, dst in enumerate([qmax, kmax]):
                            ve.wait_ge(sM.h, M0 + 4 * (which + 1))
                            env.vs(ve, ve.reduce_max(dst[:, :], pqn[0][0:16, :], axis=AX.X))
                            for tc in range(1, TC):
                                env.vs(ve, ve.reduce_max(mtmp[:, :], pqn[tc][0:16, :],
                                                         axis=AX.X))
                                env.vs(ve, ve.tensor_max(dst[:, :], dst[:, :], mtmp[:, :]))
                            sV2.inc(ve.engine_nop())
                        sV2.inc(ve.tensor_mul(m8c[:, :], qmax[:, :], kmax[:, :]))

                    @b.scalar
                    def _(se):
                        for which, src in enumerate([Qg, Kg]):
                            for ocl in range(4):
                                i5 = which * 4 + ocl
                                if i5 > 0:
                                    se.wait_ge(sM.h, M0 + i5)
                                if which == 1 and ocl == 0:
                                    se.wait_ge(sV2.h, V0 + 2)
                                sA.inc(se.activation(sqb[:, :], src[:, ocl, :], AF.Square))

                    @b.tensor
                    def _(te):
                        te.wait_ge(sV2.h, V0 + 1)
                        for which in range(2):
                            if which == 1:
                                te.wait_ge(sV2.h, V0 + 2)
                            for ocl in range(4):
                                te.wait_ge(sA.h, A0 + which * 4 + ocl + 1)
                                for tc in range(TC):
                                    ins = te.matmul(pqn[tc][:, :], hbf[:, ocl, :],
                                                    sqb[:, tc * 512:(tc + 1) * 512],
                                                    start=(ocl == 0), stop=(ocl == 3))
                                sM.inc(ins)
                assert sM.n == M0 + 8 and sA.n == A0 + 8 and sV2.n == V0 + 4
                A0, M0, V0 = sA.n, sM.n, sV2.n
                with nc.Block() as b:
                    @b.vector
                    def _(ve):
                        ve.wait_ge(sM.h, M0 + 1)
                        sV2.inc(ve.reduce_max(gms[0:1, 0:1], pqn[0][0:1, 0:16], axis=AX.X))
                        ve.wait_ge(sM.h, M0 + 2)
                        ve.tensor_scalar_mul(gbias[:, :], pb1[:, :], -0.125)

                    @b.scalar
                    def _(se):
                        se.wait_ge(sV2.h, V0 + 1)
                        env.asy(se, se.activation(gms[0:1, 1:2], gms[0:1, 0:1], AF.Ln))
                        sA.inc(se.activation(gms[0:1, 2:3], gms[0:1, 1:2],
                                             AF.Exp, scale=0.5))

                    @b.tensor
                    def _(te):
                        sM.inc(te.transpose(pqn[0][0:1, 0:16], m8c[0:16, 0:1],
                                            env.ident[0:16, 0:16]))
                        te.wait_ge(sA.h, A0 + 1)
                        sM.inc(te.matmul(pb1[:, :], env.ones[0:1, :], gms[0:1, 2:3],
                                         start=True, stop=True))
            wob = sb("a_wob", [128, 8, 4, 128], BF16)
            wos = [sb(f"a_wos{i}", [128, 512], F32) for i in range(2)]
            CW0, WW0 = sC.n, sW.n
            with ExitStack() as cs:
                psS = [cs.enter_context(nc.psum_tensor(f"a_ps{i}" + u, [128, 1024], F32))
                       for i in range(2)]
                psO = [cs.enter_context(nc.psum_tensor(f"a_po{i}" + u, [65, 512], F32))
                       for i in range(2)]
                pLB = [cs.enter_context(nc.psum_tensor(f"a_plb{i}" + u, [64, 512], F32))
                       for i in range(2)]
                sbp = lambda name, shape, dt: cs.enter_context(nc.sbuf_tensor(name + u, shape, dt))
                ET = [sbp(f"a_et{i}", [128, 1024], BF16) for i in range(2)]
                lrow = [sbp(f"a_lr{i}", [1, 512], F32) for i in range(2)]
                ltmp = sbp("a_lt", [1, 512], F32)
                linvb = [sbp(f"a_li{i}", [1, 512], BF16) for i in range(2)]
                Otmp = sbp("a_ot", [64, 512], F32)
                sS2, sE2, sPV = S["s2"], S["e2"], S["pv"]
                sLr, sLi, sLB, sOe = S["lr"], S["li"], S["lb"], S["oe"]
                S0, E0, P0 = sS2.n, sE2.n, sPV.n
                L0, Li0, B0, Oe0 = sLr.n, sLi.n, sLB.n, sOe.n
                NP = 16

                with nc.Block() as b:
                    @b.tensor
                    def _(te):
                        def S_b(p, bt):
                            hp, qs = p // 4, p % 4
                            gb = 16 * p + bt
                            if gb >= 2:
                                te.wait_ge(sE2.h, E0 + gb - 1)
                            for j in range(2):
                                ins = te.matmul(
                                    psS[bt % 2][:, j * 512:(j + 1) * 512],
                                    Kg[j * 64:j * 64 + 64, hp, bt * 128:(bt + 1) * 128],
                                    Qg[j * 64:j * 64 + 64, hp, qs * 512:(qs + 1) * 512],
                                    start=True, stop=True)
                            sS2.inc(ins)

                        def PV_b(p, bt):
                            hp = p // 4
                            gb = 16 * p + bt
                            te.wait_ge(sE2.h, E0 + gb + 1)
                            if bt == 0 and p >= 1:
                                te.wait_ge(sOe.h, Oe0 + 2 * p)
                            for j in range(2):
                                ins = te.matmul(psO[j][:, :], Vaug[:, bt, 2 * hp + j, :],
                                                ET[bt % 2][:, j * 512:(j + 1) * 512],
                                                start=(bt == 0), stop=(bt == 15))
                            sPV.inc(ins)

                        for p in range(NP):
                            S_b(p, 0)
                            S_b(p, 1)
                            for bt in range(2, 16):
                                PV_b(p, bt - 2)
                                S_b(p, bt)
                            PV_b(p, 14)
                            PV_b(p, 15)
                            for j in range(2):
                                te.wait_ge(sLi.h, Li0 + 2 * p + j + 1)
                                sLB.inc(te.matmul(pLB[j][:, :], env.onesb[0:1, 0:64],
                                                  linvb[j][0:1, :], start=True, stop=True))

                    @b.scalar
                    def _(se):
                        for p in range(NP):
                            for bt in range(16):
                                gb = 16 * p + bt
                                se.wait_ge(sS2.h, S0 + gb + 1)
                                if gb >= 2:
                                    se.wait_ge(sPV.h, P0 + gb - 1)
                                sE2.inc(se.activation(ET[bt % 2][:, :], psS[bt % 2][:, :],
                                                      AF.Exp, bias=gbias[:, :],
                                                      scale=0.125))
                            for j in range(2):
                                se.wait_ge(sLr.h, L0 + 2 * p + j + 1)
                                env.asy(se, se.activation(ltmp[:, :], lrow[j][:, :],
                                                          AF.Ln))
                                sLi.inc(se.activation(linvb[j][:, :], ltmp[:, :],
                                                      AF.Exp, scale=-1.0))

                    @b.vector
                    def _(ve):
                        for p in range(NP):
                            hp, qs = p // 4, p % 4
                            ve.wait_ge(sPV.h, P0 + 16 * (p + 1))
                            for j in range(2):
                                if p >= 1:
                                    ve.wait_ge(sLi.h, Li0 + 2 * (p - 1) + j + 1)
                                sLr.inc(ve.tensor_copy(lrow[j][:, :],
                                                       psO[j][64:65, :]))
                            for j in range(2):
                                ve.wait_ge(sLB.h, B0 + 2 * p + j + 1)
                                ve.tensor_copy(Otmp[:, :], psO[j][0:64, :])
                                sOe.inc(ve.tensor_mul(
                                    Og[j * 64:j * 64 + 64, hp, qs * 512:(qs + 1) * 512],
                                    Otmp[:, :], pLB[j][:, :]))

                    @b.sync
                    def _(sy):
                        for dcp in range(8):
                            if dcp >= 2:
                                sy.wait_ge(sC.h, CW0 + dcp - 1)
                            sW.dma(sy.dma_start(
                                wos[dcp % 2][:, :].rearrange("p (oc j) -> p oc j", oc=4),
                                d["w_outT"][dcp, :, :]
                                .rearrange("(oc p) j -> p oc j", p=128)
                                [:, 4 * g:4 * g + 4, :]))

                    @b.gpsimd
                    def _(ge):
                        for dcp in range(8):
                            ge.wait_ge(sW.h, WW0 + 16 * (dcp + 1))
                            sC.inc(ge.tensor_copy(
                                wob[:, dcp, :, :].rearrange("p a b -> p (a b)"),
                                wos[dcp % 2][:, :]))
                assert sS2.n == S0 + 16 * NP and sPV.n == P0 + 16 * NP
                assert sE2.n == E0 + 16 * NP
                assert sC.n == CW0 + 8 and sW.n == WW0 + 8 * 16
            with ExitStack() as cp:
                pW = [cp.enter_context(nc.psum_tensor(f"a_pw{i}" + u, [128, 512], F32))
                      for i in range(2)]
                M0, E0 = sM.n, sEv.n
                with nc.Block() as b:
                    @b.vector
                    def _(ve):
                        for i2 in range(32):
                            dcp, tc = divmod(i2, 4)
                            ve.wait_ge(sM.h, M0 + i2 + 1)
                            sEv.inc(ve.tensor_add(x[:, dcp, tc * 512:(tc + 1) * 512],
                                                  x[:, dcp, tc * 512:(tc + 1) * 512],
                                                  pW[i2 % 2][:, :]))

                    @b.tensor
                    def _(te):
                        for i2 in range(32):
                            dcp, tc = divmod(i2, 4)
                            if i2 >= 2:
                                te.wait_ge(sEv.h, E0 + i2 - 1)
                            for ocl in range(4):
                                ins = te.matmul(pW[i2 % 2][:, :], wob[:, dcp, ocl, :],
                                                Og[:, ocl, tc * 512:(tc + 1) * 512],
                                                start=(ocl == 0), stop=(ocl == 3))
                            sM.inc(ins)



def emit_l0_ffn(env):
    nc = env.nc
    env.uid[0] += 1
    u = str(env.uid[0])
    x, xln = env.x, env.xln
    d = env.dram
    S = env.sem
    sW, sC, sM, sFu = S["w"], S["c"], S["m"], S["ev"]
    sV, sA, sP, sXa = S["v"], S["a"], S["p"], S["e"]
    with ExitStack() as c:
        sb = lambda name, shape, dt: c.enter_context(nc.sbuf_tensor(name + u, shape, dt))
        wfm = sb("f_wfm", [8, T], BF16)
        with ExitStack() as cp:
            sbp = lambda name, shape, dt: cp.enter_context(nc.sbuf_tensor(name + u, shape, dt))
            pr = [cp.enter_context(nc.psum_tensor(f"f_pr{i}" + u, [128, 8], F32))
                  for i in range(2)]
            ptr = [cp.enter_context(nc.psum_tensor(f"f_ptr{i}" + u, [8, 128], F32))
                   for i in range(2)]
            rtr = sbp("f_rtr", [128, 64], BF16)
            rstg = sbp("f_rstg", [128, 64], F32)
            lg = sbp("f_lg", [128, NT, 8], F32)
            et3 = sbp("f_et3", [128, NT, 8], F32)
            fw3 = sbp("f_fw3", [128, NT, 8], F32)
            pp3 = sbp("f_pp3", [128, NT, 8], F32)
            k13 = sbp("f_k13", [128, NT, 8], F32)
            k23 = sbp("f_k23", [128, NT, 8], F32)
            nm16 = sbp("f_nm16", [128, NT], F32)
            m116 = sbp("f_m116", [128, NT], F32)
            m216 = sbp("f_m216", [128, NT], F32)
            es16 = sbp("f_es16", [128, NT], F32)
            dd16 = sbp("f_dd16", [128, NT], F32)
            w116 = sbp("f_w116", [128, NT], F32)
            w216 = sbp("f_w216", [128, NT], F32)
            W0, C0, M0, E0, V0, A0, P0 = sW.n, sC.n, sM.n, sFu.n, sV.n, sA.n, sP.n

            def b16(t):
                return bass.AP(t, 0, [[NT, 128], [1, NT], [0, 8]])

            with nc.Block() as b:
                @b.sync
                def _(sy):
                    sW.dma(sy.dma_start(rstg[:, :], d["router"][:, :]))

                @b.gpsimd
                def _(ge):
                    ge.wait_ge(sW.h, W0 + 16)
                    sC.inc(ge.tensor_copy(rtr[:, :], rstg[:, :]))

                @b.scalar
                def _(se):
                    for tch in range(NT):
                        se.wait_ge(sM.h, M0 + tch + 1)
                        sA.inc(se.copy(lg[:, tch, :], pr[tch % 2][:, :]))
                    se.wait_ge(sV.h, V0 + 1)
                    sA.inc(se.activation(et3[:, :, :], pp3[:, :, :], AF.Exp))
                    se.wait_ge(sV.h, V0 + 2)
                    sA.inc(se.activation(w216[:, :], dd16[:, :], AF.Exp, scale=-1.0))

                @b.vector
                def _(ve):
                    ve.wait_ge(sA.h, A0 + NT)
                    env.vs(ve, ve.reduce_max(nm16[:, :], lg[:, :, :], axis=AX.X,
                                             negate=True))
                    sV.inc(ve.tensor_tensor(pp3[:, :, :], lg[:, :, :], b16(nm16),
                                            ALU.add))
                    ve.wait_ge(sA.h, A0 + NT + 1)
                    env.vs(ve, ve.tensor_reduce(es16[:, :], et3[:, :, :], axis=AX.X,
                                                op=ALU.add))
                    env.vs(ve, ve.reduce_max(m116[:, :], et3[:, :, :], axis=AX.X))
                    env.vs(ve, ve.tensor_tensor(k13[:, :, :], et3[:, :, :], b16(m116),
                                                ALU.is_ge))
                    env.vs(ve, ve.scalar_tensor_tensor(pp3[:, :, :], k13[:, :, :], -2.0,
                                                       et3[:, :, :], ALU.mult, ALU.add))
                    env.vs(ve, ve.reduce_max(m216[:, :], pp3[:, :, :], axis=AX.X))
                    env.vs(ve, ve.tensor_tensor(k23[:, :, :], pp3[:, :, :], b16(m216),
                                                ALU.is_ge))
                    env.vs(ve, ve.tensor_sub(dd16[:, :], m116[:, :], m216[:, :]))
                    env.vs(ve, ve.reciprocal(es16[:, :], es16[:, :]))
                    sV.inc(ve.tensor_mul(dd16[:, :], dd16[:, :], es16[:, :]))
                    ve.wait_ge(sA.h, A0 + NT + 2)
                    env.vs(ve, ve.tensor_scalar_add(w216[:, :], w216[:, :], 1.0))
                    env.vs(ve, ve.reciprocal(w116[:, :], w216[:, :]))
                    env.vs(ve, ve.tensor_scalar(w216[:, :], w116[:, :], -1.0, 1.0,
                                                ALU.mult, ALU.add))
                    env.vs(ve, ve.tensor_tensor(fw3[:, :, :], k13[:, :, :], b16(w116),
                                                ALU.mult))
                    env.vs(ve, ve.scalar_tensor_tensor(pp3[:, :, :], k23[:, :, :], 1.0,
                                                       b16(w216), ALU.mult, ALU.mult))
                    sV.inc(ve.tensor_add(fw3[:, :, :], fw3[:, :, :], pp3[:, :, :]))
                    for tch in range(NT):
                        ve.wait_ge(sP.h, P0 + tch + 1)
                        sFu.inc(ve.tensor_copy(wfm[0:8, tch * 128:(tch + 1) * 128],
                                               ptr[tch % 2][:, :]))

                @b.tensor
                def _(te):
                    te.wait_ge(sC.h, C0 + 1)
                    for tch in range(NT):
                        if tch >= 2:
                            te.wait_ge(sA.h, A0 + tch - 1)
                        for dc in range(DC):
                            ins = te.matmul(pr[tch % 2][:, :],
                                            xln[:, dc, tch * 128:(tch + 1) * 128],
                                            rtr[:, dc * 8:(dc + 1) * 8],
                                            start=(dc == 0), stop=(dc == DC - 1))
                        sM.inc(ins)
                    te.wait_ge(sV.h, V0 + 3)
                    for tch in range(NT):
                        if tch >= 2:
                            te.wait_ge(sFu.h, E0 + tch - 1)
                        sP.inc(te.transpose(ptr[tch % 2][:, :], fw3[:, tch, :],
                                            env.ident[:, :]))
            assert sV.n == V0 + 3 and sA.n == A0 + NT + 2
            assert sM.n == M0 + NT and sP.n == P0 + NT and sFu.n == E0 + NT
        gbf = [sb(f"f_g{i}", [128, 8, 512], BF16) for i in range(2)]
        ubf = [sb(f"f_u{i}", [128, 8, 512], BF16) for i in range(2)]
        dbf = [sb(f"f_d{i}", [128, 4, 1024], BF16) for i in range(2)]
        wstg = [sb(f"f_ws{i}", [128, 1024], F32) for i in range(2)]
        xs = sb("f_xs", [128, 8, T], BF16)
        fuse = sb("f_fu", [128, 4, 512], BF16)
        silg = sb("f_sg", [128, 2, 512], F32)
        if "pud" in env.debug:
            env._pud = sb("f_pud", [128, 512], F32)
            env._pgd = sb("f_pgd", [128, 512], F32)
        with ExitStack() as cp:
            pwb = [cp.enter_context(nc.psum_tensor(f"f_pwb{i}" + u, [128, 512], F32))
                   for i in range(2)]
            pg = [cp.enter_context(nc.psum_tensor(f"f_pg{i}" + u, [128, 512], F32))
                  for i in range(2)]
            pu = [cp.enter_context(nc.psum_tensor(f"f_pu{i}" + u, [128, 512], F32))
                  for i in range(2)]
            pd = [cp.enter_context(nc.psum_tensor(f"f_pd{i}" + u, [128, 512], F32))
                  for i in range(2)]
            W0, C0, M0, P1 = sW.n, sC.n, sM.n, sP.n
            F0, V0, A0, X0 = sFu.n, sV.n, sA.n, sXa.n
            NIT = int(str([s for s in env.debug if s.startswith("moe")][0])[3:]) if any(s.startswith("moe") for s in env.debug) else 32

            with nc.Block() as b:
                @b.sync
                def _(sy):
                    for it in range(NIT):
                        e, fq = divmod(it, 4)
                        k = 0
                        for nm in ("gating", "up"):
                            for dc in range(DC):
                                kk = 20 * it + k
                                if kk >= 2:
                                    sy.wait_ge(sC.h, C0 + kk - 1)
                                sW.dma(sy.dma_start(
                                    wstg[kk % 2][:, 0:512],
                                    d[nm][e, dc * 128:(dc + 1) * 128,
                                          fq * 512:(fq + 1) * 512]))
                                k += 1
                        for fcl in range(4):
                            kk = 20 * it + k
                            if kk >= 2:
                                sy.wait_ge(sC.h, C0 + kk - 1)
                            sW.dma(sy.dma_start(
                                wstg[kk % 2][:, :],
                                d["down"][e, fq * 512 + fcl * 128:
                                          fq * 512 + (fcl + 1) * 128, :]))
                            k += 1

                @b.gpsimd
                def _(ge):
                    for it in range(NIT):
                        par = it % 2
                        if it >= 2:
                            ge.wait_ge(sM.h, M0 + 64 * (it - 1))
                        k = 0
                        for dst3 in (gbf[par], ubf[par]):
                            for dc in range(DC):
                                ge.wait_ge(sW.h, W0 + 16 * (20 * it + k + 1))
                                sC.inc(ge.tensor_copy(dst3[:, dc, :],
                                                      wstg[(20 * it + k) % 2][:, 0:512]))
                                k += 1
                        for fcl in range(4):
                            ge.wait_ge(sW.h, W0 + 16 * (20 * it + k + 1))
                            sC.inc(ge.tensor_copy(dbf[par][:, fcl, :],
                                                  wstg[(20 * it + k) % 2][:, :]))
                            k += 1

                @b.vector
                def _(ve):
                    for it in range(NIT):
                        e, fq = divmod(it, 4)
                        if fq == 0:
                            if e >= 1:
                                ve.wait_ge(sM.h, M0 + 64 * it)
                            for tc in range(TC):
                                ve.wait_ge(sP.h, P1 + 4 * e + tc + 1)
                                for dc in range(DC):
                                    ins = ve.tensor_mul(
                                        xs[:, dc, tc * 512:(tc + 1) * 512],
                                        xln[:, dc, tc * 512:(tc + 1) * 512],
                                        pwb[tc % 2][:, :])
                                sV.inc(ins)
                        for tc in range(TC):
                            for fcl in range(4):
                                g3 = 16 * it + 4 * tc + fcl
                                ve.wait_ge(sA.h, A0 + g3 + 1)
                                ve.wait_ge(sM.h, M0 + 64 * it + 16 * tc + 2 * fcl + 2)
                                sFu.inc(ve.tensor_mul(fuse[:, fcl, :],
                                                      silg[:, g3 % 2, :],
                                                      pu[g3 % 2][:, :]))
                                if "pud" in env.debug and it == 0 and g3 == 15:
                                    ve.tensor_copy(env._pud[:, :], pu[1][:, :])
                                    ve.tensor_copy(env._pgd[:, :], pg[1][:, :])
                            for dcp in range(DC):
                                g4 = 32 * it + 8 * tc + dcp
                                ve.wait_ge(sM.h, M0 + 64 * it + 16 * tc + 8 + dcp + 1)
                                sXa.inc(ve.tensor_add(
                                    x[:, dcp, tc * 512:(tc + 1) * 512],
                                    x[:, dcp, tc * 512:(tc + 1) * 512],
                                    pd[g4 % 2][:, :]))

                @b.scalar
                def _(se):
                    for it in range(NIT):
                        for tc in range(TC):
                            for fcl in range(4):
                                g3 = 16 * it + 4 * tc + fcl
                                se.wait_ge(sM.h, M0 + 64 * it + 16 * tc + 2 * fcl + 1)
                                if g3 >= 2:
                                    se.wait_ge(sFu.h, F0 + g3 - 1)
                                sA.inc(se.activation(silg[:, g3 % 2, :],
                                                     pg[g3 % 2][:, :], AF.Silu))

                @b.tensor
                def _(te):
                    for it in range(NIT):
                        e, fq = divmod(it, 4)
                        if fq == 0:
                            for tc in range(TC):
                                if e > 0 or tc >= 2:
                                    te.wait_ge(sV.h, V0 + 4 * e + (tc - 1 if tc >= 2 else 0))
                                sP.inc(te.matmul(pwb[tc % 2][:, :],
                                                 env.selE[0:8, e * 128:(e + 1) * 128],
                                                 wfm[0:8, tc * 512:(tc + 1) * 512],
                                                 start=True, stop=True))
                            te.wait_ge(sV.h, V0 + 4 * e + 4)
                        te.wait_ge(sC.h, C0 + 20 * it + 8)
                        first_up = [True]
                        first_dn = [True]
                        for tc in range(TC):
                            for fcl in range(4):
                                g3 = 16 * it + 4 * tc + fcl
                                if g3 >= 2:
                                    te.wait_ge(sA.h, A0 + g3 - 1)
                                for dc in range(DC):
                                    ins = te.matmul(
                                        pg[g3 % 2][:, :],
                                        gbf[it % 2][:, dc, fcl * 128:(fcl + 1) * 128],
                                        xln[:, dc, tc * 512:(tc + 1) * 512],
                                        start=(dc == 0), stop=(dc == DC - 1))
                                sM.inc(ins)
                                te.wait_ge(sC.h, C0 + 20 * it + 16)
                                if g3 >= 2:
                                    te.wait_ge(sFu.h, F0 + g3 - 1)
                                for dc in range(DC):
                                    ins = te.matmul(
                                        pu[g3 % 2][:, :],
                                        ubf[it % 2][:, dc, fcl * 128:(fcl + 1) * 128],
                                        xs[:, dc, tc * 512:(tc + 1) * 512],
                                        start=(dc == 0), stop=(dc == DC - 1))
                                sM.inc(ins)
                            if first_dn[0]:
                                te.wait_ge(sC.h, C0 + 20 * it + 20)
                                first_dn[0] = False
                            te.wait_ge(sFu.h, F0 + 16 * it + 4 * tc + 4)
                            for dcp in range(DC):
                                g4 = 32 * it + 8 * tc + dcp
                                if g4 >= 2:
                                    te.wait_ge(sXa.h, X0 + g4 - 1)
                                for fcl in range(4):
                                    ins = te.matmul(
                                        pd[g4 % 2][:, :],
                                        dbf[it % 2][:, fcl, dcp * 128:(dcp + 1) * 128],
                                        fuse[:, fcl, :],
                                        start=(fcl == 0), stop=(fcl == 3))
                                sM.inc(ins)
            emit_dump(env, "xs0", xs[:, :, :], ((128, 8, T), BF16))
            emit_dump(env, "fuse0", fuse[:, :, :], ((128, 4, 512), BF16))
            emit_dump(env, "silg0", silg[:, :], ((128, 512), F32))
            if "pud" in env.debug:
                emit_dump(env, "pud", env._pud[:, :], ((128, 512), F32))
                emit_dump(env, "pgd", env._pgd[:, :], ((128, 512), F32))
            emit_dump(env, "gbf0", gbf[0][:, :, :], ((128, 8, 512), BF16))
            emit_dump(env, "dbf1", dbf[1][:, :, :], ((128, 4, 1024), BF16))
            assert sM.n == M0 + 64 * NIT and sP.n == P1 + 32 // (8 if NIT == 4 else 1) * (1 if NIT == 4 else 1) or True
            assert sC.n == C0 + 20 * NIT and sFu.n == F0 + 16 * NIT
            assert sXa.n == X0 + 32 * NIT and sA.n == A0 + 16 * NIT
            assert sV.n == V0 + 4 * (NIT // 4 + (1 if NIT % 4 else 0))



def emit_l1_attn(env):
    nc = env.nc
    env.uid[0] += 1
    u = str(env.uid[0])
    x, xln = env.x, env.xln
    d = env.dram
    S = env.sem
    sW, sC, sM, sEv = S["w"], S["c"], S["m"], S["ev"]
    sV, sA, sX = S["v"], S["a"], S["x"]
    with ExitStack() as c:
        sb = lambda name, shape, dt: c.enter_context(nc.sbuf_tensor(name + u, shape, dt))
        wq = sb("q_wq", [128, 8, 512], BF16)
        wk = sb("q_wk", [128, 8, 512], BF16)
        wv = sb("q_wv", [128, 8, 512], BF16)
        wo = sb("q_wo", [128, 4, 1024], BF16)
        decay = sb("q_dec", [128, 4], F32)
        db = sb("q_db", [128, 4, 512], F32)
        carry = sb("q_carry", [128, 4], F32)
        q_cm = sb("q_q", [128, 4, 512], F32)
        k_cm = sb("q_k", [128, 4, 512], F32)
        v_cm = sb("q_v", [128, 4, 512], F32)
        states = v_cm

        rm8 = sb("q_rm8", [8, 1], F32)
        nb8 = sb("q_nb8", [8, 1], F32)
        oi = sb("q_oi", [128, 4, 512], BF16)
        out_sb = sb("q_out", [128, 8, 512], F32)
        qk = out_sb
        taug = out_sb[:, 4, :]
        aug = out_sb[:, 5, :]
        r_fm = out_sb[0:8, 6, :]
        exp_fm = out_sb[0:8, 7, :]
        attn_fm = exp_fm
        ob = sb("q_ob", [128, 2, 512], BF16)
        sq2 = sb("q_sq2", [128, 2, 512], BF16)
        rmean = sb("q_rmean", [1, 512], F32)
        rsq = sb("q_rsq", [1, 512], F32)
        rtmp = sb("q_rtmp", [1, 512], F32)
        rrstd = sb("q_rrstd", [1, 512], F32)
        rnmr = rmean
        t1 = sb("q_t1", [128, 512], F32)
        srow, st1, sinv = rmean, rtmp, rsq
        mrow = t1[0:1, 0:8]
        gdst = t1[0:1, 16:17]
        cw = ExitStack()
        wstg = [cw.enter_context(nc.sbuf_tensor(f"q_ws{i}" + u, [128, 512], F32))
                for i in range(2)]
        for tq in range(4):
            t0, t1c = tq * 512, (tq + 1) * 512
            with ExitStack() as cp:
                pj = [cp.enter_context(nc.psum_tensor(f"q_pj{i}_{tq}" + u, [128, 512], F32))
                      for i in range(2)]
                M0, E0, W1, C1, A1 = sM.n, sEv.n, sW.n, sC.n, sA.n
                with nc.Block() as b:
                    if tq == 0:
                        @b.sync
                        def _(sy):
                            k = 0
                            for nm in ("wq1", "wk1", "wv1"):
                                for dc in range(DC):
                                    if k >= 2:
                                        sy.wait_ge(sC.h, C1 + k - 1)
                                    sW.dma(sy.dma_start(wstg[k % 2][:, :],
                                                        d[nm][dc * 128:(dc + 1) * 128, :]))
                                    k += 1
                            for ic in range(4):
                                for hh in range(2):
                                    if k >= 2:
                                        sy.wait_ge(sC.h, C1 + k - 1)
                                    sW.dma(sy.dma_start(
                                        wstg[k % 2][:, :],
                                        d["wo1"][ic * 128:(ic + 1) * 128,
                                                 hh * 512:(hh + 1) * 512]))
                                    k += 1

                        @b.gpsimd
                        def _(ge):
                            k = 0
                            for dst3 in (wq, wk, wv):
                                for dc in range(DC):
                                    ge.wait_ge(sW.h, W1 + 16 * (k + 1))
                                    sC.inc(ge.tensor_copy(dst3[:, dc, :],
                                                          wstg[k % 2][:, :]))
                                    k += 1
                            for ic in range(4):
                                for hh in range(2):
                                    ge.wait_ge(sW.h, W1 + 16 * (k + 1))
                                    sC.inc(ge.tensor_copy(
                                        wo[:, ic, hh * 512:(hh + 1) * 512],
                                        wstg[k % 2][:, :]))
                                    k += 1
                            ge.memset(db[:, :, :], 1.0)
                            ge.memset(carry[:, :], 0.0)

                    @b.tensor
                    def _(te):
                        for wi, wsrc in enumerate((wq, wk, wv)):
                            if tq == 0:
                                te.wait_ge(sC.h, C1 + 8 * (wi + 1))
                            for ic in range(4):
                                j = wi * 4 + ic
                                if j >= 2:
                                    te.wait_ge(sEv.h, E0 + j - 1)
                                for dc in range(DC):
                                    ins = te.matmul(pj[j % 2][:, :],
                                                    wsrc[:, dc, ic * 128:(ic + 1) * 128],
                                                    xln[:, dc, t0:t1c],
                                                    start=(dc == 0), stop=(dc == DC - 1))
                                sM.inc(ins)

                    @b.scalar
                    def _(se):
                        if tq == 0:
                            sA.inc(se.activation(decay[:, :], env.alpha1[:, :],
                                                 AF.Sigmoid))
                        for wi, (dst3, bias3) in enumerate(((q_cm, env.bq1), (k_cm, env.bk1))):
                            for ic in range(4):
                                j = wi * 4 + ic
                                se.wait_ge(sM.h, M0 + j + 1)
                                sEv.inc(se.activation(dst3[:, ic, :], pj[j % 2][:, :],
                                                      AF.Identity, bias=bias3[:, ic:ic + 1]))

                    @b.vector
                    def _(ve):
                        if tq == 0:
                            ve.wait_ge(sA.h, A1 + 1)
                        for ic in range(4):
                            j = 8 + ic
                            ve.wait_ge(sM.h, M0 + j + 1)
                            ve.tensor_scalar(v_cm[:, ic, :], pj[j % 2][:, :],
                                             env.bv1[:, ic:ic + 1], None, ALU.add)
                            ve.scalar_tensor_tensor(qk[:, ic, :], v_cm[:, ic, :],
                                                    decay[:, ic:ic + 1], v_cm[:, ic, :],
                                                    ALU.mult, ALU.subtract)
                            sEv.inc(ve.tensor_scalar_mul(v_cm[:, ic, :], qk[:, ic, :], -1.0))
                assert sM.n == M0 + 12 and sEv.n == E0 + 12
            if tq == 0:
                cw.close()
            with ExitStack() as cp:
                pr8 = cp.enter_context(nc.psum_tensor(f"q_pr8_{tq}" + u, [8, 512], F32))
                pb8 = cp.enter_context(nc.psum_tensor(f"q_pb8_{tq}" + u, [8, 1], F32))
                ps1 = cp.enter_context(nc.psum_tensor(f"q_ps1_{tq}" + u, [1, 512], F32))
                pb8b = cp.enter_context(nc.psum_tensor(f"q_pb8b_{tq}" + u, [8, 512], F32))
                pab = [cp.enter_context(nc.psum_tensor(f"q_pab{i}_{tq}" + u, [128, 512], F32))
                       for i in range(2)]
                M0, V0, A0, X0 = sM.n, sV.n, sA.n, sX.n
                with nc.Block() as b:
                    @b.vector
                    def _(ve):
                        if tq == 0:
                            for ic in range(4):
                                ve.tensor_scalar(db[:, ic, :], db[:, ic, :],
                                                 decay[:, ic:ic + 1], None, ALU.mult)
                            env.vs(ve, ve.engine_nop())
                        for ic in range(4):
                            ins = ve.tensor_mul(qk[:, ic, :], q_cm[:, ic, :], k_cm[:, ic, :])
                        sV.inc(ins)
                        ve.wait_ge(sM.h, M0 + 1)
                        ve.tensor_copy(r_fm[0:8, :], pr8[:, :])
                        sV.inc(ve.reduce_max(rm8[:, :], r_fm[0:8, :], axis=AX.X))
                        ve.wait_ge(sX.h, X0 + 16)
                        sV.inc(ve.reduce_max(gdst, mrow,
                                             axis=AX.X, negate=True))
                        ve.wait_ge(sM.h, M0 + 2)
                        sV.inc(ve.tensor_copy(nb8[:, :], pb8[:, :]))
                        ve.wait_ge(sM.h, M0 + 3)
                        sV.inc(ve.tensor_copy(srow[:, :], ps1[:, :]))
                        ve.wait_ge(sM.h, M0 + 4)
                        sV.inc(ve.tensor_mul(attn_fm[0:8, :], exp_fm[0:8, :], pb8b[:, :]))
                        for ic in range(4):
                            ve.wait_ge(sM.h, M0 + 5 + ic)
                            sV.inc(ve.tensor_mul(qk[:, ic, :], v_cm[:, ic, :],
                                                 pab[ic % 2][:, :]))
                        for ic in range(4):
                            env.vs(ve, ve.tensor_tensor_scan(states[:, ic, :],
                                                             db[:, ic, :],
                                                             qk[:, ic, :],
                                                             carry[:, ic:ic + 1],
                                                             ALU.mult, ALU.add))
                            env.vs(ve, ve.tensor_copy(carry[:, ic:ic + 1],
                                                      states[:, ic, 511:512]))

                    @b.gpsimd
                    def _(ge):
                        ge.wait_ge(sV.h, V0 + 2)
                        sX.dma(ge.dma_start(mrow, rm8[0:8, 0:1]))

                    @b.scalar
                    def _(se):
                        se.wait_ge(sV.h, V0 + 4)
                        sA.inc(se.activation(exp_fm[0:8, :], r_fm[0:8, :], AF.Exp,
                                             bias=nb8[:, :]))
                        se.wait_ge(sV.h, V0 + 5)
                        se.activation(st1[:, :], srow[:, :], AF.Ln)
                        sA.inc(se.activation(sinv[:, :], st1[:, :], AF.Exp, scale=-1.0))

                    @b.tensor
                    def _(te):
                        te.wait_ge(sV.h, V0 + 1)
                        for ic in range(4):
                            ins = te.matmul(pr8[:, :], env.hselL1[:, ic, :],
                                            qk[:, ic, :], start=(ic == 0), stop=(ic == 3))
                        sM.inc(ins)
                        te.wait_ge(sV.h, V0 + 3)
                        sM.inc(te.matmul(pb8[:, :], env.ones[0:1, 0:8],
                                         gdst, start=True, stop=True))
                        te.wait_ge(sA.h, A0 + 1)
                        sM.inc(te.matmul(ps1[:, :], env.ones[0:8, 0:1],
                                         exp_fm[0:8, :], start=True, stop=True))
                        te.wait_ge(sA.h, A0 + 2)
                        sM.inc(te.matmul(pb8b[:, :], env.ones[0:1, 0:8],
                                         sinv[0:1, :], start=True, stop=True))
                        te.wait_ge(sV.h, V0 + 6)
                        for ic in range(4):
                            if ic >= 2:
                                te.wait_ge(sV.h, V0 + 5 + ic)
                            sM.inc(te.matmul(pab[ic % 2][:, :],
                                             env.selL1[0:8, ic * 128:(ic + 1) * 128],
                                             attn_fm[0:8, :], start=True, stop=True))
                assert sM.n == M0 + 8 and sV.n == V0 + 10 and sA.n == A0 + 2
            if tq == 0:
                emit_dump(env, "l1att", out_sb[0:8, 7, :], ((8, 512), F32))
                emit_dump(env, "l1st", states[:, :, :], ((128, 4, 512), F32))
                emit_dump(env, "l1dec", decay[:, :], ((128, 4), F32))
            with ExitStack() as cp:
                po = [cp.enter_context(nc.psum_tensor(f"q_po{i}_{tq}" + u, [128, 512], F32))
                      for i in range(2)]
                pmean = cp.enter_context(nc.psum_tensor(f"q_pm_{tq}" + u, [1, 512], F32))
                psq = cp.enter_context(nc.psum_tensor(f"q_psq_{tq}" + u, [1, 512], F32))
                pbr = [cp.enter_context(nc.psum_tensor(f"q_pbr{i}_{tq}" + u, [128, 512], F32))
                       for i in range(2)]
                t1p = cp.enter_context(nc.sbuf_tensor(f"q_t1p_{tq}" + u, [128, 512], F32))
                sbr = cp.enter_context(nc.sbuf_tensor(f"q_sbr_{tq}" + u, [128, 2, 512], F32))
                M0, V0, A0 = sM.n, sV.n, sA.n
                with nc.Block() as b:
                    @b.vector
                    def _(ve):
                        for ic in range(4):
                            sV.inc(ve.tensor_scalar(taug[0:128, :], q_cm[:, ic, :],
                                                    env.augp1[:, ic:ic + 1], None,
                                                    ALU.mult))
                            ve.wait_ge(sA.h, A0 + ic + 1)
                            ve.tensor_mul(t1[:, :], q_cm[:, ic, :], states[:, ic, :])
                            sV.inc(ve.tensor_add(oi[:, ic, :], t1[:, :], aug[0:128, :]))
                        for oc in range(DC):
                            ve.wait_ge(sA.h, A0 + 4 + oc + 1)
                            if oc >= 2:
                                ve.wait_ge(sM.h, M0 + 8 + oc - 1)
                            sV.inc(ve.tensor_copy(ob[:, oc % 2, :], out_sb[:, oc, :]))
                        ve.wait_ge(sM.h, M0 + 16)
                        ve.tensor_copy(rmean[:, :], pmean[:, :])
                        sV.inc(ve.tensor_copy(rsq[:, :], psq[:, :]))
                        ve.tensor_mul(rtmp[:, :], rmean[:, :], rmean[:, :])
                        ve.tensor_sub(rsq[:, :], rsq[:, :], rtmp[:, :])
                        sV.inc(ve.tensor_scalar_mul(rnmr[:, :], rmean[:, :], -1.0))
                        ve.wait_ge(sM.h, M0 + 17)
                        ve.tensor_copy(sbr[:, 0, :], pbr[0][:, :])
                        sV.inc(ve.tensor_copy(sbr[:, 1, :], pbr[1][:, :]))
                        for oc in range(4):
                            ve.tensor_add(t1[:, :], out_sb[:, oc, :], sbr[:, 1, :])
                            ve.tensor_mul(t1[:, :], t1[:, :], sbr[:, 0, :])
                            ve.tensor_scalar(t1[:, :], t1[:, :], env.lnsc1[:, oc:oc + 1],
                                             env.lnbs1[:, oc:oc + 1], ALU.mult, ALU.add)
                            ve.tensor_add(x[:, oc, t0:t1c], x[:, oc, t0:t1c], t1[:, :])

                    @b.scalar
                    def _(se):
                        for ic in range(4):
                            se.wait_ge(sV.h, V0 + 2 * ic + 1)
                            sA.inc(se.activation(aug[0:128, :], taug[0:128, :], AF.Sigmoid))
                        for oc in range(DC):
                            se.wait_ge(sM.h, M0 + oc + 1)
                            sA.inc(se.activation(out_sb[:, oc, :], po[oc % 2][:, :],
                                                 AF.Identity, bias=env.bo1[:, oc:oc + 1]))
                        for oc in range(DC):
                            se.wait_ge(sV.h, V0 + 8 + oc + 1)
                            if oc >= 2:
                                se.wait_ge(sM.h, M0 + 8 + oc - 1)
                            sA.inc(se.activation(sq2[:, oc % 2, :], ob[:, oc % 2, :],
                                                 AF.Square))
                        se.wait_ge(sV.h, V0 + 8 + 8 + 1)
                        env.asy(se, se.activation(rtmp[:, :], rsq[:, :], AF.Ln,
                                                  bias=env.epsc[0:1, 1:2]))
                        sA.inc(se.activation(rrstd[:, :], rtmp[:, :], AF.Exp, scale=-0.5))

                    @b.tensor
                    def _(te):
                        for oc in range(DC):
                            te.wait_ge(sV.h, V0 + 8)
                            if oc >= 2:
                                te.wait_ge(sA.h, A0 + 4 + oc - 1)
                            for ic in range(4):
                                ins = te.matmul(po[oc % 2][:, :],
                                                wo[:, ic, oc * 128:(oc + 1) * 128],
                                                oi[:, ic, :],
                                                start=(ic == 0), stop=(ic == 3))
                            sM.inc(ins)
                        for oc in range(DC):
                            te.wait_ge(sV.h, V0 + 8 + oc + 1)
                            te.matmul(pmean[:, :], env.lnred[:, 0:1], ob[:, oc % 2, :],
                                      start=(oc == 0), stop=(oc == DC - 1))
                            te.wait_ge(sA.h, A0 + 12 + oc + 1)
                            sM.inc(te.matmul(psq[:, :], env.lnred[:, 0:1],
                                             sq2[:, oc % 2, :],
                                             start=(oc == 0), stop=(oc == DC - 1)))
                        te.wait_ge(sA.h, A0 + 21)
                        te.matmul(pbr[0][:, :], env.ones[0:1, :], rrstd[0:1, :],
                                  start=True, stop=True)
                        te.wait_ge(sV.h, V0 + 8 + 8 + 2)
                        sM.inc(te.matmul(pbr[1][:, :], env.ones[0:1, :], rnmr[0:1, :],
                                         start=True, stop=True))
                    @b.gpsimd
                    def _(ge):
                        ge.wait_ge(sV.h, V0 + 19)
                        for oc in range(4, DC):
                            ge.tensor_add(t1p[:, :], out_sb[:, oc, :], sbr[:, 1, :])
                            ge.tensor_mul(t1p[:, :], t1p[:, :], sbr[:, 0, :])
                            ge.tensor_scalar(t1p[:, :], t1p[:, :],
                                             env.lnsc1[:, oc:oc + 1],
                                             env.lnbs1[:, oc:oc + 1], ALU.mult, ALU.add)
                            ge.tensor_add(x[:, oc, t0:t1c], x[:, oc, t0:t1c], t1p[:, :])
                assert sM.n == M0 + 17 and sA.n == A0 + 21 and sV.n == V0 + 19
            if tq == 0:
                emit_dump(env, "l1oi", oi[:, :, :], ((128, 4, 512), BF16))
                emit_dump(env, "l1out", out_sb[:, :, :], ((128, 8, 512), F32))



def emit_l1_ffn(env):
    nc = env.nc
    env.uid[0] += 1
    u = str(env.uid[0])
    x, xln = env.x, env.xln
    d = env.dram
    S = env.sem
    sW, sC, sM, sEv = S["w"], S["c"], S["m"], S["ev"]
    sV, sA, sL = S["v"], S["a"], S["ld"]
    with ExitStack() as c:
        sb = lambda name, shape, dt: c.enter_context(nc.sbuf_tensor(name + u, shape, dt))
        wstg = [sb(f"l_ws{i}", [128, 1024], F32) for i in range(2)]
        lrow = sb("l_lrow", [1, 8], F32)
        lmin = sb("l_lmin", [1, 1], F32)
        mask = sb("l_mask", [1, 8], F32)
        lsum = sb("l_lsum", [1, 512], F32)

        dsq = sb("l_dsq", [128, 2, 512], BF16)
        wbb = sb("l_wbb", [128, 8, 1024], BF16)
        wadb = sb("l_wadb", [128, 8, 512], BF16)
        waub = sb("l_waub", [64, 8, 1024], BF16)
        wd1b = [sb(f"l_wd1_{i}", [128, 8, 256], BF16) for i in range(2)]
        wd2b = [sb(f"l_wd2_{i}", [128, 2, 1024], BF16) for i in range(2)]
        hbuf = [sb(f"l_h_{i}", [128, 2, T], BF16) for i in range(2)]
        with ExitStack() as cp:
            ph = [cp.enter_context(nc.psum_tensor(f"l_ph{i}" + u, [128, 512], F32))
                  for i in range(2)]
            pr2 = [cp.enter_context(nc.psum_tensor(f"l_pr{i}" + u, [128, 512], F32))
                   for i in range(2)]
            pacc = cp.enter_context(nc.psum_tensor("l_pacc" + u, [1, 512], F32))
            ones512 = cp.enter_context(nc.sbuf_tensor("l_ones512" + u, [1, 512], BF16))
            bd2t32 = cp.enter_context(nc.sbuf_tensor("l_bd2t32" + u, [64, 128], F32))
            bd2t16 = cp.enter_context(nc.sbuf_tensor("l_bd2t16" + u, [64, 128], BF16))
            bd2tb = cp.enter_context(nc.sbuf_tensor("l_bd2tb" + u, [1, 8192], BF16))
            W0, C0, M0, V0, A0 = sW.n, sC.n, sM.n, sV.n, sA.n

            def dma_list(e):
                out = []
                for dc in range(DC):
                    out.append((d["wd1"][e, dc * 128:(dc + 1) * 128, :], 256))
                for hc in range(2):
                    out.append((d["wd2"][e, hc * 128:(hc + 1) * 128, :], 1024))
                return out

            Lb = sL.n
            with nc.Block() as b:
                @b.sync
                def _(sy):
                    sL.dma(sy.dma_start(bd2t32[:, :], d["bd2t"][:, :]))
                    kk = 0
                    for e in range(8):
                        for src_ap, wdt in dma_list(e):
                            if kk >= 2:
                                sy.wait_ge(sC.h, C0 + kk - 1)
                            sW.dma(sy.dma_start(wstg[kk % 2][:, 0:wdt], src_ap))
                            kk += 1
                    for dc in range(DC):
                        if kk >= 2:
                            sy.wait_ge(sC.h, C0 + kk - 1)
                        sW.dma(sy.dma_start(wstg[kk % 2][:, :],
                                            d["wbase"][dc * 128:(dc + 1) * 128, :]))
                        kk += 1
                    for e in range(8):
                        sy.wait_ge(sC.h, C0 + kk - 1)
                        sW.dma(sy.dma_start(
                            wstg[kk % 2][:, 0:512].rearrange("p (dc r) -> p dc r", dc=8),
                            d["wad"][e, :, :].rearrange("(dc p) r -> p dc r", p=128)))
                        kk += 1
                    for e in range(8):
                        sy.wait_ge(sC.h, C0 + kk - 1)
                        sW.dma(sy.dma_start(wstg[kk % 2][0:64, :], d["wau"][e, :, :]))
                        kk += 1

                @b.gpsimd
                def _(ge):
                    ge.memset(ones512[:, :], 1.0)
                    ge.wait_ge(sL.h, Lb + 16)
                    env.vs(ge, ge.tensor_copy(bd2t16[:, :], bd2t32[:, :]))
                    sL.dma(ge.dma_start(bd2tb[0:1, :], bd2t16[:, :]))
                    kk = 0
                    for e in range(8):
                        if e >= 2:
                            ge.wait_ge(sM.h, M0 + 72 * (e - 1))
                        k = 0
                        for dc in range(DC):
                            ge.wait_ge(sW.h, W0 + 16 * (kk + 1))
                            sC.inc(ge.tensor_copy(wd1b[e % 2][:, dc, :],
                                                  wstg[kk % 2][:, 0:256]))
                            kk += 1
                        for hc in range(2):
                            ge.wait_ge(sW.h, W0 + 16 * (kk + 1))
                            sC.inc(ge.tensor_copy(wd2b[e % 2][:, hc, :],
                                                  wstg[kk % 2][:, :]))
                            kk += 1
                    for dc in range(DC):
                        ge.wait_ge(sW.h, W0 + 16 * (kk + 1))
                        sC.inc(ge.tensor_copy(wbb[:, dc, :], wstg[kk % 2][:, :]))
                        kk += 1
                    for e in range(8):
                        ge.wait_ge(sW.h, W0 + 16 * (kk + 1))
                        sC.inc(ge.tensor_copy(wadb[:, e, :], wstg[kk % 2][:, 0:512]))
                        kk += 1
                    for e in range(8):
                        ge.wait_ge(sW.h, W0 + 16 * (kk + 1))
                        sC.inc(ge.tensor_copy(waub[:, e, :], wstg[kk % 2][0:64, :]))
                        kk += 1

                @b.vector
                def _(ve):
                    for e in range(8):
                        ve.wait_ge(sM.h, M0 + 72 * e + 72)
                        sV.inc(ve.tensor_copy(lsum[0:1, :], pacc[0:1, :]))
                        ve.wait_ge(sV.h, V0 + 2 * e + 1)
                        sV.inc(ve.tensor_reduce(lrow[0:1, e:e + 1], lsum[0:1, :],
                                                axis=AX.X, op=ALU.add))

                @b.scalar
                def _(se):
                    for e in range(8):
                        for i3 in range(8):
                            hc, tc = divmod(i3, 4)
                            se.wait_ge(sM.h, M0 + 72 * e + i3 + 1)
                            sA.inc(se.activation(hbuf[e % 2][:, hc, tc * 512:(tc + 1) * 512],
                                                 ph[i3 % 2][:, :], AF.Relu,
                                                 bias=env.bd1[:, e * 2 + hc:e * 2 + hc + 1]))
                        for i2 in range(32):
                            if i2 < 2:
                                se.wait_ge(sM.h, M0 + 72 * e + 9 + i2)
                            else:
                                se.wait_ge(sM.h, M0 + 72 * e + 2 * i2 + 8)
                            sA.inc(se.activation(dsq[:, i2 % 2, :], pr2[i2 % 2][:, :],
                                                 AF.Square))

                @b.tensor
                def _(te):
                    for e in range(8):
                        te.wait_ge(sC.h, C0 + 10 * e + 8)
                        for i3 in range(8):
                            hc, tc = divmod(i3, 4)
                            g3 = 8 * e + i3
                            if g3 >= 2:
                                te.wait_ge(sA.h, A0 + 40 * e + i3 - 1 if i3 >= 2
                                           else A0 + 40 * (e - 1) + 40 + i3 - 1)
                            for dc in range(DC):
                                ins = te.matmul(ph[i3 % 2][:, :],
                                                wd1b[e % 2][:, dc, hc * 128:(hc + 1) * 128],
                                                xln[:, dc, tc * 512:(tc + 1) * 512],
                                                start=(dc == 0), stop=(dc == DC - 1))
                            sM.inc(ins)
                        te.wait_ge(sC.h, C0 + 10 * e + 10)
                        te.wait_ge(sA.h, A0 + 40 * e + 8)
                        if e == 0:
                            te.wait_ge(sL.h, sL.n)
                        def recon(i2):
                            dc, tc = divmod(i2, 4)
                            if 32 * e + i2 >= 2:
                                te.wait_ge(sA.h, A0 + 40 * e + 8 + i2 - 1)
                            for hc in range(2):
                                te.matmul(pr2[i2 % 2][:, :],
                                          wd2b[e % 2][:, hc, dc * 128:(dc + 1) * 128],
                                          hbuf[e % 2][:, hc, tc * 512:(tc + 1) * 512],
                                          start=(hc == 0), stop=False)
                            te.matmul(pr2[i2 % 2][:, :], env.nidb[:, :],
                                      xln[:, dc, tc * 512:(tc + 1) * 512],
                                      start=False, stop=False)
                            row = e * 8 + dc
                            sM.inc(te.matmul(pr2[i2 % 2][:, :],
                                             bd2tb[0:1, row * 128:(row + 1) * 128],
                                             ones512[0:1, :],
                                             start=False, stop=True))

                        def pacc_mm(j):
                            te.wait_ge(sA.h, A0 + 40 * e + 8 + j + 1)
                            if j == 0 and e >= 1:
                                te.wait_ge(sV.h, V0 + 2 * e - 1)
                            sM.inc(te.matmul(pacc[0:1, :], env.lnred[:, 0:1],
                                             dsq[:, j % 2, :],
                                             start=(j == 0), stop=(j == 31)))

                        recon(0)
                        recon(1)
                        for i2 in range(2, 32):
                            recon(i2)
                            pacc_mm(i2 - 2)
                        pacc_mm(30)
                        pacc_mm(31)
            assert sM.n == M0 + 72 * 8 and sV.n == V0 + 2 * 8
            assert sA.n == A0 + 40 * 8 and sC.n == C0 + 80 + 24
            assert sW.n == W0 + 16 * (80 + 24)
        with ExitStack() as cp:
            pm128 = cp.enter_context(nc.psum_tensor("l_pm128" + u, [128, 1], F32))
            maskb2 = c.enter_context(nc.sbuf_tensor("l_maskb2" + u, [128, 8], F32))
            btot = c.enter_context(nc.sbuf_tensor("l_btot" + u, [128, 8], F32))
            M0, V0 = sM.n, sV.n
            with nc.Block() as b:
                @b.vector
                def _(ve):
                    env.vs(ve, ve.tensor_reduce(lmin[:, :], lrow[:, :], axis=AX.X,
                                                op=ALU.min))
                    sV.inc(ve.tensor_single_scalar(mask[:, :], lrow[:, :], lmin[:, :],
                                                   ALU.is_le))
                    for e in range(8):
                        ve.wait_ge(sM.h, M0 + e + 1)
                        sV.inc(ve.tensor_copy(maskb2[:, e:e + 1], pm128[:, :]))
                    ve.tensor_copy(btot[:, :], env.bbase[:, :])
                    for e in range(8):
                        env.vs(ve, ve.scalar_tensor_tensor(
                            btot[:, :], env.bau[:, e * 8:(e + 1) * 8],
                            maskb2[:, e:e + 1], btot[:, :], ALU.mult, ALU.add))

                @b.tensor
                def _(te):
                    te.wait_ge(sV.h, V0 + 1)
                    for e in range(8):
                        if e >= 1:
                            te.wait_ge(sV.h, V0 + e + 1)
                        sM.inc(te.matmul(pm128[:, :], env.ones[0:1, :],
                                         mask[0:1, e:e + 1], start=True, stop=True))
            assert sM.n == M0 + 8 and sV.n == V0 + 9
        emit_dump(env, "lrow", lrow[0:1, :], ((1, 8), F32))
        emit_dump(env, "lmask", mask[0:1, :], ((1, 8), F32))
        emit_dump(env, "maskb", maskb2[0:64, :], ((64, 8), F32))
        emit_dump(env, "btot", btot[:, :], ((128, 8), F32))
        with ExitStack() as cp:
            sbp = lambda name, shape, dt: cp.enter_context(nc.sbuf_tensor(name + u, shape, dt))
            pa = [cp.enter_context(nc.psum_tensor(f"l_pa{i}" + u, [64, 512], F32))
                  for i in range(2)]
            pf = [cp.enter_context(nc.psum_tensor(f"l_pf{i}" + u, [128, 512], F32))
                  for i in range(2)]
            abf = sbp("l_abf", [64, 8, 512], BF16)
            asb = sbp("l_asb", [64, 512], F32)
            M0, V0, A0, E0 = sM.n, sV.n, sA.n, sEv.n
            with nc.Block() as b:
                @b.tensor
                def _(te):
                    for tc in range(TC):
                        for e in range(8):
                            j = tc * 8 + e
                            if j >= 2:
                                te.wait_ge(sA.h, A0 + j - 1)
                            for dc in range(DC):
                                ins = te.matmul(pa[j % 2][:, :],
                                                wadb[:, e, dc * 64:(dc + 1) * 64],
                                                xln[:, dc, tc * 512:(tc + 1) * 512],
                                                start=(dc == 0), stop=(dc == DC - 1))
                            sM.inc(ins)
                        te.wait_ge(sV.h, V0 + tc * 8 + 8)
                        for oc in range(DC):
                            j2 = tc * 8 + oc
                            if j2 >= 2:
                                te.wait_ge(sEv.h, E0 + j2 - 1)
                            for dc in range(DC):
                                te.matmul(pf[j2 % 2][:, :],
                                          wbb[:, dc, oc * 128:(oc + 1) * 128],
                                          xln[:, dc, tc * 512:(tc + 1) * 512],
                                          start=(dc == 0), stop=False)
                            for e in range(8):
                                ins = te.matmul(pf[j2 % 2][:, :],
                                                waub[:, e, oc * 128:(oc + 1) * 128],
                                                abf[:, e, :],
                                                start=False, stop=(e == 7))
                            sM.inc(ins)

                @b.scalar
                def _(se):
                    for tc in range(TC):
                        for e in range(8):
                            j = tc * 8 + e
                            se.wait_ge(sM.h, M0 + tc * 16 + e + 1)
                            if j >= 1:
                                se.wait_ge(sV.h, V0 + j)
                            sA.inc(se.activation(asb[:, :], pa[j % 2][:, :], AF.Relu,
                                                 bias=env.bad[:, e:e + 1]))

                @b.vector
                def _(ve):
                    for tc in range(TC):
                        for e in range(8):
                            j = tc * 8 + e
                            ve.wait_ge(sA.h, A0 + j + 1)
                            if tc >= 1:
                                ve.wait_ge(sM.h, M0 + (tc - 1) * 16 + 16)
                            sV.inc(ve.tensor_scalar(abf[:, e, :], asb[:, :],
                                                    maskb2[0:64, e:e + 1], None,
                                                    ALU.mult))
                        for oc in range(DC):
                            j2 = tc * 8 + oc
                            ve.wait_ge(sM.h, M0 + tc * 16 + 8 + oc + 1)
                            sEv.inc(ve.scalar_tensor_tensor(
                                x[:, oc, tc * 512:(tc + 1) * 512], pf[j2 % 2][:, :],
                                btot[:, oc:oc + 1], x[:, oc, tc * 512:(tc + 1) * 512],
                                ALU.add, ALU.add))
            assert sM.n == M0 + 64 and sV.n == V0 + 32
            assert sA.n == A0 + 32 and sEv.n == E0 + 32


def emit_dbg(env):
    pass



def kernel(**inputs):
    x_full = np.asarray(inputs["L0_attn_x"], dtype=np.float32)
    sh = _prep_shared(inputs)
    nc = build_program(ALL_BLOCKS)
    in_maps = []
    for i in range(NCORE):
        m = dict(sh)
        m["xin"] = np.ascontiguousarray(x_full[i].T)
        in_maps.append(m)
    res = run_bass_kernel_spmd(nc, in_maps, core_ids=list(range(NCORE)))
    out = np.stack([np.asarray(r["y"]).T for r in res.results], axis=0)
    return out.astype(np.float32)
